# version 07 (final)
# baseline (speedup 1.0000x reference)
import math

import numpy as np
import ml_dtypes
import concourse.bass as bass
import concourse.mybir as mybir
from concourse.bass_utils import run_bass_kernel_spmd

F32 = mybir.dt.float32
BF16 = mybir.dt.bfloat16
I32 = mybir.dt.int32
ALU = mybir.AluOpType
ACTF = mybir.ActivationFunctionType
AXX = mybir.AxisListType.X
BF = ml_dtypes.bfloat16

E = 1024
HPC = 8
T = 4096
TH = 2048
NTTH = 16
NSP = 8
NKC = 32
NSPAN = 8
NTT = 32
EPS = 1e-5
REPL = [[0, 1], [2, 3], [4, 5], [6, 7]]
PASS3 = ((0, 0), (0, 1), (1, 0))
LS = 256

ENGS = ("sync", "gpsimd", "tensor", "scalar", "vector")


class Graph:
    def __init__(self):
        self.ops = {e: [] for e in ENGS}
        self.nsem = {}

    def op(self, eng, fn, waits=(), sem=None):
        tok = None
        if sem is not None:
            v = self.nsem.get(sem, 0) + 1
            self.nsem[sem] = v
            tok = (sem, v, 1)
        self.ops[eng].append((fn, tuple(w for w in waits if w), tok))
        return (sem, tok[1]) if tok else None

    def dma(self, eng, dst, src, waits=(), sp=False):
        q = "q_" + eng
        v = self.nsem.get(q, 0) + 16
        self.nsem[q] = v
        self.ops[eng].append(
            (lambda e, d=dst, s=src, sp=sp: e.dma_start(d, s, single_packet=sp)
             if sp else e.dma_start(d, s),
             tuple(w for w in waits if w), (q, v, 16)))
        return (q, v)

    def emit(self, eng, name, sems):
        cur = {}
        for fn, waits, tok in self.ops[name]:
            for w in waits:
                sn, sv = w[0], w[1]
                force = len(w) > 2 and w[2]
                if force or cur.get(sn, 0) < sv:
                    eng.wait_ge(sems[sn], sv)
                    if cur.get(sn, 0) < sv:
                        cur[sn] = sv
            inst = fn(eng)
            if tok is not None:
                sn, sv, inc = tok
                if inst is not None and hasattr(inst, "then_inc"):
                    inst.then_inc(sems[sn], inc)
                else:
                    eng.sem_inc(sems[sn], inc)


def FW(tok):
    return (tok[0], tok[1], True) if tok else None


NO_QKV_LORA = False


def build_nc(debug_outs=(), dbg_l=0):
    nc = bass.Bass("TRN2", target_bir_lowering=False)

    def kind_of(name):
        return "ExternalOutput" if name in debug_outs else "Internal"

    d_x = nc.dram_tensor("x", [T, E], F32, kind="ExternalInput")
    d_xown = nc.dram_tensor("xown", [TH, E], F32, kind="ExternalInput")
    d_wqT = nc.dram_tensor("wqT", [E, 512], F32, kind="ExternalInput")
    d_wkT = nc.dram_tensor("wkT", [E, 512], F32, kind="ExternalInput")
    d_wvT = nc.dram_tensor("wvT", [E, 512], F32, kind="ExternalInput")
    d_bqkvT = nc.dram_tensor("bqkvT", [E, 24], F32, kind="ExternalInput")
    d_aqkvT = nc.dram_tensor("aqkvT", [3, 8, 512], F32, kind="ExternalInput")
    d_wpT = nc.dram_tensor("wpT", [512, E], F32, kind="ExternalInput")
    d_qT = nc.dram_tensor("qT6", [6, E, E], I32, kind="ExternalInput")
    d_scT = nc.dram_tensor("scT6", [6, E, E], F32, kind="ExternalInput")
    d_bias = nc.dram_tensor("bias6", [6, 128, E], F32, kind="ExternalInput")
    d_laT = nc.dram_tensor("laT6", [6, E, 32], F32, kind="ExternalInput")
    d_lbT = nc.dram_tensor("lbT6", [6, 32, E], F32, kind="ExternalInput")
    d_gam = nc.dram_tensor("gam", [3, 128, E], F32, kind="ExternalInput")
    d_bet = nc.dram_tensor("bet", [3, 128, E], F32, kind="ExternalInput")
    d_nslope = nc.dram_tensor("nslope", [128, HPC], F32, kind="ExternalInput")
    d_out = nc.dram_tensor("out", [TH, E], F32, kind="ExternalOutput")

    d_kb = nc.dram_tensor("kb_scr", [HPC, 128, T], BF16, kind=kind_of("kb_scr"))
    d_m = nc.dram_tensor("m_scr", [HPC, 128, T], BF16, kind=kind_of("m_scr"))
    d_vhi = nc.dram_tensor("vhi_scr", [T, 512], BF16, kind=kind_of("vhi_scr"))
    d_vlo = nc.dram_tensor("vlo_scr", [T, 512], BF16, kind=kind_of("vlo_scr"))
    d_o = nc.dram_tensor("o_scr", [T, 512], F32, kind=kind_of("o_scr"))
    d_ccin = nc.dram_tensor("cc_in", [T, E], F32, kind=kind_of("cc_in"))
    d_ccout = nc.dram_tensor("cc_out", [TH, E], F32, kind=kind_of("cc_out"))
    d_h = nc.dram_tensor("h_scr", [3, TH, E], F32, kind=kind_of("h_scr"))
    d_ldbg = None
    if "ldbg" in debug_outs:
        d_ldbg = nc.dram_tensor("ldbg", [HPC * NTT, 128], F32, kind="ExternalOutput")
    d_pdbg = None
    if "pdbg" in debug_outs:
        d_pdbg = nc.dram_tensor("pdbg", [128, NKC * 512], F32, kind="ExternalOutput")
    d_vdbg = None
    if "vdbg" in debug_outs:
        d_vdbg = nc.dram_tensor("vdbg", [128, 2 * NKC * 129], BF16, kind="ExternalOutput")
    d_fdbg = None
    if "fdbg" in debug_outs:
        d_fdbg = nc.dram_tensor("fdbg", [3, TH, E], F32, kind="ExternalOutput")
    d_wfdbg = None
    if "wfdbg" in debug_outs:
        d_wfdbg = nc.dram_tensor("wfdbg", [128, 2 * 8 * 1024], BF16, kind="ExternalOutput")
    d_ccdbg = None
    if "ccdbg" in debug_outs:
        d_ccdbg = nc.dram_tensor("ccdbg", [TH, E], F32, kind="ExternalOutput")
    d_lndbg = None
    if "lndbg" in debug_outs:
        d_lndbg = nc.dram_tensor("lndbg", [18, 128, E], F32, kind="ExternalOutput")
        d_lnst = nc.dram_tensor("lnst", [16, 128, 16], F32, kind="ExternalOutput")
    d_podbg = None
    if "podbg" in debug_outs:
        d_podbg = nc.dram_tensor("podbg", [2, 128, 129], F32, kind="ExternalOutput")
    d_xtdbg = None
    if "xtdbg" in debug_outs:
        d_xtdbg = nc.dram_tensor("xtdbg", [2, 128, 4096], BF16, kind="ExternalOutput")
    d_wdbg = None
    if "wdbg" in debug_outs:
        d_wdbg = nc.dram_tensor("wdbg", [2, 128, 8192], BF16, kind="ExternalOutput")

    cur = [16384]
    OFFS = {}

    def at(name, shape, dtype, base=None):
        if base is not None:
            cur[0] = base
        nb = int(np.prod(shape[1:])) * mybir.dt.size(dtype)
        off = (cur[0] + 31) // 32 * 32
        cur[0] = off + nb
        OFFS[name] = off
        return nc.alloc_sbuf_tensor_at(name, list(shape), dtype, offset=off)

    ident = at("ident", [128, 128], BF16)
    bjt = at("bjt", [128, HPC * NKC], F32)
    jio = at("jio", [128, NKC], F32)
    slrep = at("slrep", [128, HPC], F32)
    epst = at("epst", [128, 1], F32)
    eps2t = at("eps2t", [128, 1], F32)
    onesb = at("onesb", [128, 128], BF16)
    zconst = at("zconst", [128, 1], F32)
    zconstb = at("zconstb", [128, 1], BF16)
    nc.const_aps.register(F32, 0.0, zconst[:, :])
    nc.const_aps.register(BF16, 0.0, zconstb[:, :])
    PBASE = ((cur[0] + 63) // 64) * 64

    xbuf = at("xbuf", [128, 2 * 4096], F32, base=PBASE)
    xnbuf = at("xnbuf", [128, 2 * 4096], F32)
    sq1 = at("sq1", [128, 1024], F32)
    dhi = at("dhi", [128, 2 * 1024], BF16)
    dlo = at("dlo", [128, 2 * 1024], BF16)
    xnT_hi = at("xnT_hi", [128, 8 * 512], BF16)
    xnT_lo = at("xnT_lo", [128, 8 * 512], BF16)
    wq_s = at("wq_s", [128, 2 * 8 * 512], BF16)
    wk_s = at("wk_s", [128, 2 * 8 * 512], BF16)
    wv_s = at("wv_s", [128, 2 * 8 * 512], BF16)
    wstg = at("wstg", [128, 2 * 512], F32)
    qevac = at("qevac", [128, 2 * 512], F32)
    mshi = at("mshi", [128, 2 * 512], BF16)
    mslo = at("mslo", [128, 2 * 512], BF16)
    lnt = at("lnt", [128, 16], F32)
    U = at("U", [128, 2 * T], BF16)
    AZ3 = at("AZ3", [128, 3 * 1024], BF16)
    ASTG = at("ASTG", [128, 512], F32)
    BQK = at("BQK", [128, 2 * 8 * 24], BF16)
    BSTG = at("BSTG", [128, 8 * 24], F32)
    ustg = at("ustg", [24, 512], F32)
    ubf_hi = at("ubf_hi", [24, 512], BF16)
    ubf_lo = at("ubf_lo", [24, 512], BF16)
    P1_END = cur[0]

    kbb = at("kbb", [128, 2 * T], BF16, base=PBASE)
    mb = at("mb", [128, 2 * T], BF16)
    m2b = at("m2b", [128, 2 * T], BF16)
    vbb = at("vbb", [128, 2 * NKC * 129], BF16)
    pthi = at("pthi", [128, NKC * 512], BF16)
    ptlo = at("ptlo", [128, NKC * 512], BF16)
    pfr = at("pfr", [128, 3 * 512], F32)
    nsc = at("nsc", [128, 8], F32)
    otile = at("otile", [128, 2 * 64], F32)
    otmp = at("otmp", [128, 256], F32)
    P2_END = cur[0]

    inT_hi = at("inT_hi", [128, 8 * 2048], BF16, base=PBASE)
    inT_lo = at("inT_lo", [128, 8 * 2048], BF16)
    wf = at("wf", [128, 2 * 2 * 8 * 1024], BF16)
    qstg = at("qstg", [128, 1024], I32)
    screp = at("screp", [128, 2 * 1024], F32)
    wdeq = at("wdeq", [128, 2 * 1024], F32)
    uf = at("uf", [128, 2 * 2048], BF16)
    ufstg = at("ufstg", [32, 512], F32)
    lastg = at("lastg", [128, 2 * 32], F32)
    laT_s = at("laT_s", [128, 2 * 8 * 32], BF16)
    lbstg = at("lbstg", [32, 256], F32)
    lbT_s = at("lbT_s", [128, 2 * 1024], BF16)
    brep = at("brep", [128, 1024], F32)
    grep = at("grep", [128, 1024], F32)
    berep = at("berep", [128, 1024], F32)
    hring = at("hring", [128, 1024], F32)
    yev = at("yev", [128, 2 * 1024], F32)
    zbuf = at("zbuf", [128, 1024], F32)
    xrld = at("xrld", [128, 1024], F32)
    ccrld = nc.alloc_sbuf_tensor_at("ccrld", [128, 1024], F32, offset=OFFS["zbuf"])
    fdhi = at("fdhi", [128, 1024], BF16)
    fdlo = at("fdlo", [128, 1024], BF16)
    outb = nc.alloc_sbuf_tensor_at("outb", [128, 1024], F32, offset=OFFS["fdhi"])
    fsq = at("fsq", [128, 1024], F32)
    fln = at("fln", [128, 16], F32)
    wp_s = at("wp_s", [128, 2 * 4 * 1024], BF16, base=PBASE)
    wpstg = at("wpstg", [128, 1024], F32)
    orld = at("orld", [128, 2 * 512], F32)
    obhi = at("obhi", [128, 2 * 512], BF16)
    oblo = at("oblo", [128, 2 * 512], BF16)
    oThi = at("oThi", [128, 2 * 512], BF16)
    oTlo = at("oTlo", [128, 2 * 512], BF16)
    apev = at("apev", [128, 2 * 512], F32)
    P3A_END = cur[0]
    assert P3A_END - PBASE <= 2 * 8 * 2048 * 2, P3A_END
    P3_END = cur[0]
    print("ENDS:", P1_END, P2_END, P3_END)
    assert max(P1_END, P2_END, P3_END) <= 229376, (P1_END, P2_END, P3_END)

    ps_a = nc.alloc_psum_tensor("ps_a", [128, 512], F32)
    ps_b = nc.alloc_psum_tensor("ps_b", [128, 512], F32)
    ps_c = nc.alloc_psum_tensor("ps_c", [128, 512], F32)
    ps_d = nc.alloc_psum_tensor("ps_d", [128, 512], F32)
    ps_u = nc.alloc_psum_tensor("ps_u", [128, 512], F32)
    ps_w = nc.alloc_psum_tensor("ps_w", [128, 512], F32)
    ps_tr = nc.alloc_psum_tensor("ps_tr", [128, 1024], BF16)
    ps_t2 = nc.alloc_psum_tensor("ps_t2", [128, 1024], BF16)
    PRING = [ps_a, ps_b, ps_c, ps_d]

    G = Graph()
    op, dma = G.op, G.dma

    def mm(waits, out_ap, lhsT, rhs, start, stop, sem=None, tp=None):
        def fn(t, o=out_ap, l=lhsT, r=rhs, s=start, st=stop, tpos=tp):
            if tpos is None:
                return t.matmul(o, l, r, start=s, stop=st)
            return t.matmul(o, l, r, start=s, stop=st, tile_position=tpos)
        return op("tensor", fn, waits, sem=sem)

    LNLAST = [None]

    def ln_emit(xin, xout, scr, lnb, par, waits_in, tag, out_gates=()):
        s_sum = lnb[:, 0 + par:1 + par]
        s_ssq = lnb[:, 4 + par:5 + par]
        s_std = lnb[:, 6 + par:7 + par]
        s_rst = lnb[:, 8 + par:9 + par]
        vt = "v_" + tag
        at_ = "a_" + tag
        pt_ = "p_" + tag
        t1 = op("vector", lambda v: v.reduce_sum(s_sum, xin, axis=AXX),
                list(waits_in) + [LNLAST[0]], sem=vt)
        t2 = op("vector", lambda v: v.tensor_scalar_mul(xout, xin, float(E)),
                list(out_gates) + [FW(w) for w in waits_in if w], sem=vt)
        t3 = op("gpsimd", lambda g: g.tensor_sub(
            xout, xout, bass.AP(lnb, 0 + par, [[16, 128], [0, 1024]])),
            [t1, t2], sem=pt_)
        t4 = op("scalar", lambda a: a.activation(scr, xout, ACTF.Square,
                                                 accum_out=s_ssq),
                [t3, tok_zc, tok_zcb], sem=at_)
        op("scalar", lambda a: a.activation(lnb[:, 14:15], epst[:, :], ACTF.Copy),
           [], sem=at_)
        op("scalar", lambda a: a.activation(lnb[:, 15:16], epst[:, :], ACTF.Copy),
           [], sem=at_)
        t5 = op("scalar", lambda a: a.activation(
            s_std, s_ssq, ACTF.Sqrt, bias=eps2t[:, :], scale=1.0 / float(E)),
            [("p_set", 3), tok_ep2, FW(t4)], sem=at_)
        t6 = op("vector", lambda v: v.reciprocal(s_rst, s_std), [t5], sem=vt)
        t7 = op("gpsimd", lambda g: g.tensor_mul(
            xout, xout, bass.AP(lnb, 8 + par, [[16, 128], [0, 1024]])),
            [t6], sem=pt_)
        LNLAST[0] = t7
        return t3, t7

    tok_ones = op("gpsimd", lambda g: g.memset(onesb[:], 1.0), [], sem="p_set")
    tok_id = op("gpsimd", lambda g: g.affine_select(
        ident[:], onesb[:], [[-1, 128]], ALU.is_equal, 0.0,
        base=0, channel_multiplier=1), [], sem="p_set")
    tok_eps = op("gpsimd", lambda g: g.memset(epst[:], EPS), [], sem="p_set")
    assert tok_eps == ("p_set", 3)
    tok_ep2 = op("gpsimd", lambda g: g.memset(eps2t[:], float(E) * float(E) * EPS),
                 [], sem="p_set")
    tok_jio = op("gpsimd", lambda g: g.iota(
        jio[:], [[128, NKC]], base=0, channel_multiplier=1,
        allow_small_or_imprecise_dtypes=True), [], sem="p_set")
    tok_zc = op("gpsimd", lambda g: g.memset(zconst[:], 0.0), [], sem="p_set")
    tok_zcb = op("gpsimd", lambda g: g.memset(zconstb[:], 0.0), [], sem="p_set")
    vb4 = vbb[:].rearrange("p (b k c) -> p b k c", b=2, k=NKC)

    tok_sl = dma("sync", slrep[:, :], bass.AP(d_nslope, 0, [[HPC, 128], [1, HPC]]))
    tok_bj = None
    for hh in range(HPC):
        tok_bj = op("vector", lambda v, hh=hh: v.tensor_scalar_mul(
            bjt[:, hh * 32:hh * 32 + 32], jio[:], slrep[:, hh:hh + 1]),
            [tok_jio, tok_sl], sem="v_bj")

    wlo = []
    for wi, (dw, wdst) in enumerate(((d_wqT, wq_s), (d_wkT, wk_s), (d_wvT, wv_s))):
        for c in range(8):
            idx = wi * 8 + c
            tl = dma("sync", wstg[:, (idx % 2) * 512:(idx % 2) * 512 + 512],
                     bass.AP(dw, c * 128 * 512, [[512, 128], [1, 512]]),
                     [wlo[idx - 2] if idx >= 2 else None])
            th = op("vector", lambda v, wdst=wdst, c=c, idx=idx: v.tensor_copy(
                wdst[:, c * 512:c * 512 + 512],
                wstg[:, (idx % 2) * 512:(idx % 2) * 512 + 512]), [tl], sem="v_ws")
            tlo = op("gpsimd", lambda g, wdst=wdst, c=c, idx=idx: g.tensor_sub(wdst[:, 4096 + c * 512:4096 + c * 512 + 512],
                wstg[:, (idx % 2) * 512:(idx % 2) * 512 + 512], wdst[:, c * 512:c * 512 + 512]),
                [th], sem="p_ws")
            wlo.append(tlo)
    tb = dma("sync", BSTG[:, :],
             bass.AP(d_bqkvT, 0, [[24, 128], [128 * 24, 8], [1, 24]]))
    tbh = op("vector", lambda v: v.tensor_copy(BQK[:, 0:192], BSTG[:, :]),
             [tb], sem="v_ws")
    tbl = op("gpsimd", lambda g: g.tensor_sub(BQK[:, 192:384], BSTG[:, :], BQK[:, 0:192]),
        [tbh], sem="p_ws")
    tok_az = op("gpsimd", lambda g: g.memset(AZ3[:], 0.0), [], sem="p_set")
    tok_um = op("gpsimd", lambda g: g.memset(U[:], 0.0), [], sem="p_set")
    ta_l = [dma("sync", ASTG[32 * i:32 * i + 8, :],
                bass.AP(d_aqkvT, i * 8 * 512, [[512, 8], [1, 512]]))
            for i in range(3)]
    tal = None
    for i, r0 in enumerate((0, 32, 64)):
        th_ = op("vector", lambda v, i=i, r0=r0: v.tensor_copy(
            AZ3[r0:r0 + 8, i * 1024:i * 1024 + 512], ASTG[r0:r0 + 8, :]),
            ta_l + [tok_az], sem="v_ws")
        tal = op("gpsimd", lambda g, i=i, r0=r0: g.tensor_sub(
            AZ3[r0:r0 + 8, i * 1024 + 512:i * 1024 + 1024], ASTG[r0:r0 + 8, :],
            AZ3[r0:r0 + 8, i * 1024:i * 1024 + 512]), [th_], sem="p_ws")
    w_ready = [wlo[-1], tbl, tal]

    xc_tok = [None] * NTT
    te_tok = []
    pe_tok = []
    ms_tok = []
    scr4 = []
    ue_a = [None] * NSP
    ue_dma = [None] * NSP

    for ts in range(NSP):
        ld = []
        for tt in range(4):
            gi = ts * 4 + tt
            xo = (ts % 2) * 4096 + tt * 1024
            ld.append(dma("sync", xbuf[:, xo:xo + 1024],
                          bass.AP(d_x, gi * 128 * E, [[E, 128], [1, E]]),
                          [xc_tok[gi - 8] if gi >= 8 else None]))
        xn_t = []
        for tt in range(4):
            gi = ts * 4 + tt
            xo = (ts % 2) * 4096 + tt * 1024
            xc, xn = ln_emit(xbuf[:, xo:xo + 1024], xnbuf[:, xo:xo + 1024],
                             sq1[:, :], lnt, gi % 2, [ld[tt]], "p1")
            xc_tok[gi] = xc
            xn_t.append(xn)
        for tt in range(4):
            gi = ts * 4 + tt
            xo = (ts % 2) * 4096 + tt * 1024
            dsl = (gi % 2) * 1024
            th = op("vector", lambda v, xo=xo, dsl=dsl: v.tensor_copy(
                dhi[:, dsl:dsl + 1024], xnbuf[:, xo:xo + 1024]),
                [xn_t[tt], te_tok[2 * (gi - 2) + 1] if gi >= 2 else None],
                sem="v_p1")
            tl = op("vector", lambda v, xo=xo, dsl=dsl: v.scalar_tensor_tensor(
                dlo[:, dsl:dsl + 1024], xnbuf[:, xo:xo + 1024], 1.0,
                dhi[:, dsl:dsl + 1024], ALU.mult, ALU.subtract), [FW(th)], sem="v_p1")
            prev_e = te_tok[2 * (gi - 1) + 1] if gi >= 1 else None
            tr1 = None
            for c in range(8):
                tr1 = op("tensor", lambda t, c=c, dsl=dsl: t.transpose(
                    ps_tr[:, c * 128:c * 128 + 128],
                    dhi[:, dsl + c * 128:dsl + c * 128 + 128], ident[:]),
                    ([th, tok_id, prev_e] if c == 0 else []) + [FW(th)],
                    sem="t_tr" if c == 7 else None)
            tr2 = None
            for c in range(8):
                tr2 = op("tensor", lambda t, c=c, dsl=dsl: t.transpose(
                    ps_t2[:, c * 128:c * 128 + 128],
                    dlo[:, dsl + c * 128:dsl + c * 128 + 128], ident[:]),
                    ([tl] if c == 0 else []) + [FW(tl)],
                    sem="t_tr" if c == 7 else None)
            xh = xnT_hi[:].rearrange("p (c w) -> p c w", w=512)[:, :, tt * 128:tt * 128 + 128]
            xl = xnT_lo[:].rearrange("p (c w) -> p c w", w=512)[:, :, tt * 128:tt * 128 + 128]
            psh = ps_tr[:].rearrange("p (c w) -> p c w", w=128)
            psl = ps_t2[:].rearrange("p (c w) -> p c w", w=128)
            te1 = op("scalar", lambda a, xh=xh, psh=psh: a.activation(xh, psh, ACTF.Copy),
                     [tr1], sem="a_te")
            te2 = op("scalar", lambda a, xl=xl, psl=psl: a.activation(xl, psl, ACTF.Copy),
                     [tr2], sem="a_te")
            te_tok.extend([te1, te2])

        strip_ready = te_tok[-1]
        if d_xtdbg is not None and ts == NSP - 1:
            dma("gpsimd", bass.AP(d_xtdbg, 0, [[4096, 128], [1, 4096]]),
                xnT_hi[:], [strip_ready])
            dma("gpsimd", bass.AP(d_xtdbg, 128 * 4096, [[4096, 128], [1, 4096]]),
                xnT_lo[:], [strip_ready])
        if d_wdbg is not None and ts == NSP - 1:
            dma("gpsimd", bass.AP(d_wdbg, 0, [[8192, 128], [1, 8192]]),
                wk_s[:], [strip_ready])
            dma("gpsimd", bass.AP(d_wdbg, 128 * 8192, [[8192, 128], [1, 8192]]),
                wv_s[:], [strip_ready])
        first = True
        lastm = None
        for c in range(8):
            for (sl, ml) in PASS3:
                lastm = mm(([strip_ready, ue_a[ts - 1] if ts >= 1 else None,
                             *w_ready] if first else [])
                           + ([FW(strip_ready)] if (sl, ml) == PASS3[0] else []),
                           bass.AP(ps_u, 0, [[512, 24], [1, 512]]),
                           BQK[:, sl * 192 + c * 24:sl * 192 + c * 24 + 24],
                           (xnT_hi if ml == 0 else xnT_lo)[:, c * 512:c * 512 + 512],
                           first, (c == 7 and (sl, ml) == (1, 0)),
                           sem="t_pq" if (c == 7 and (sl, ml) == (1, 0)) else None)
                first = False
        aue = op("scalar", lambda a: a.activation(
            ustg[:, :], bass.AP(ps_u, 0, [[512, 24], [1, 512]]), ACTF.Copy,
            scale=2.0), [lastm, ue_dma[ts - 1] if ts >= 1 else None], sem="a_ue")
        ue_a[ts] = aue
        vue = op("vector", lambda v: v.tensor_copy(ubf_hi[:, :], ustg[:, :]),
                 [aue], sem="v_ue")
        pue = op("gpsimd", lambda g: g.tensor_sub(ubf_lo[:, :], ustg[:, :], ubf_hi[:, :]),
            [vue], sem="p_ue")
        ut = None
        for i, r0 in enumerate((0, 32, 64)):
            dma("sync", U[r0:r0 + 8, ts * 512:ts * 512 + 512],
                ubf_hi[8 * i:8 * i + 8, :], [vue, tok_um])
            ut = dma("sync", U[r0:r0 + 8, T + ts * 512:T + ts * 512 + 512],
                     ubf_lo[8 * i:8 * i + 8, :], [pue])
        ue_dma[ts] = ut

        for j in range(12):
            ip = ts * 12 + j
            pb = PRING[ip % 4]
            psum_gate = pe_tok[ip - 4] if ip >= 4 else None
            first = True
            lastm = None
            if j < 8:
                oc = j % 4
                ws_, r0 = (wq_s, 0) if j < 4 else (wk_s, 32)
                for c in range(8):
                    cw = [FW(strip_ready)]
                    for (sl, ml) in PASS3:
                        lastm = mm(([psum_gate, ue_dma[ts]] if first else []) + cw,
                                   pb[:],
                                   ws_[:, sl * 4096 + c * 512 + oc * 128:sl * 4096 + c * 512 + oc * 128 + 128],
                                   (xnT_hi if ml == 0 else xnT_lo)[:, c * 512:c * 512 + 512],
                                   first, False)
                        cw = []
                        first = False
                az = 0 if j < 4 else 1
                for pi, (asl, usl) in enumerate(PASS3):
                    lastm = mm([FW(ue_dma[ts])] if pi == 0 else [], pb[:],
                               AZ3[:, az * 1024 + asl * 512 + oc * 128:az * 1024 + asl * 512 + oc * 128 + 128],
                               U[:, usl * T + ts * 512:usl * T + ts * 512 + 512],
                               False, pi == 2,
                               sem="t_pq" if pi == 2 else None)
            else:
                tt = j - 8
                for (sl, ml) in PASS3:
                    for c in range(8):
                        lastm = mm(([psum_gate, ue_dma[ts]] if first else [])
                                   + ([FW(strip_ready)] if c == 0 else []),
                                   pb[:],
                                   (xnT_hi if sl == 0 else xnT_lo)[:, c * 512 + tt * 128:c * 512 + tt * 128 + 128],
                                   wv_s[:, ml * 4096 + c * 512:ml * 4096 + c * 512 + 512],
                                   first, False)
                        first = False
                for pi, (usl, asl) in enumerate(PASS3):
                    lastm = mm([FW(ue_dma[ts])] if pi == 0 else [], pb[:],
                               U[:, usl * T + ts * 512 + tt * 128:usl * T + ts * 512 + tt * 128 + 128],
                               AZ3[:, 2 * 1024 + asl * 512:2 * 1024 + asl * 512 + 512],
                               False, pi == 2,
                               sem="t_pq" if pi == 2 else None)
            scale = 0.125 if j < 4 else 1.0
            ape = op("scalar", lambda a, pb=pb, ip=ip, scale=scale: a.activation(
                qevac[:, (ip % 2) * 512:(ip % 2) * 512 + 512], pb[:], ACTF.Copy,
                scale=scale),
                [lastm, ms_tok[ip - 2][1] if ip >= 2 else None], sem="a_pe")
            pe_tok.append(ape)
            vms = op("vector", lambda v, ip=ip: v.tensor_copy(
                mshi[:, (ip % 2) * 512:(ip % 2) * 512 + 512],
                qevac[:, (ip % 2) * 512:(ip % 2) * 512 + 512]),
                [ape, scr4[ip - 2] if ip >= 2 else None], sem="v_ms")
            pms = op("gpsimd", lambda g, ip=ip: g.tensor_sub(mslo[:, (ip % 2) * 512:(ip % 2) * 512 + 512],
                qevac[:, (ip % 2) * 512:(ip % 2) * 512 + 512], mshi[:, (ip % 2) * 512:(ip % 2) * 512 + 512]), [vms], sem="p_ms")
            ms_tok.append((vms, pms))
            sl_ = ip % 2
            if j < 8:
                dscr = d_m if j < 4 else d_kb
                oc = j % 4
                h0, h1 = 2 * oc, 2 * oc + 1
                dma("sync", bass.AP(dscr, h0 * 128 * T + ts * 512, [[T, 64], [1, 512]]),
                    mshi[0:64, sl_ * 512:sl_ * 512 + 512], [vms])
                dma("sync", bass.AP(dscr, h1 * 128 * T + ts * 512, [[T, 64], [1, 512]]),
                    mshi[64:128, sl_ * 512:sl_ * 512 + 512], [])
                dma("sync", bass.AP(dscr, h0 * 128 * T + 64 * T + ts * 512, [[T, 64], [1, 512]]),
                    mslo[0:64, sl_ * 512:sl_ * 512 + 512], [pms])
                last = dma("sync", bass.AP(dscr, h1 * 128 * T + 64 * T + ts * 512, [[T, 64], [1, 512]]),
                           mslo[64:128, sl_ * 512:sl_ * 512 + 512], [])
            else:
                gi = ts * 4 + (j - 8)
                dma("sync", bass.AP(d_vhi, gi * 128 * 512, [[512, 128], [1, 512]]),
                    mshi[:, sl_ * 512:sl_ * 512 + 512], [vms])
                last = dma("sync", bass.AP(d_vlo, gi * 128 * 512, [[512, 128], [1, 512]]),
                           mslo[:, sl_ * 512:sl_ * 512 + 512], [pms])
            scr4.append(last)
    p1_done = scr4[-1]

    CUMS = [0]
    for _s in range(NSPAN):
        CUMS.append(CUMS[-1] + 4 * _s + 4)

    hd_tok = [None] * HPC
    ex_tok = []
    dec_tok = []
    nm_tok = {}
    ost_tok = {}

    tok_vo0 = op("gpsimd", lambda g: g.memset(vb4[:, 0, :, 64:65], 1.0),
                 [p1_done], sem="p_set")
    tok_vo1 = op("gpsimd", lambda g: g.memset(vb4[:, 1, :, 64:65], 1.0),
                 [], sem="p_set")
    for h in range(HPC):
        bo = h % 2
        g1 = [p1_done if h == 0 else None,
              hd_tok[h - 2] if h >= 2 else None]
        l1 = dma("sync", kbb[:, bo * T:bo * T + T],
                 bass.AP(d_kb, h * 128 * T, [[T, 128], [1, T]]), g1)
        l2 = dma("sync", mb[:, bo * T:bo * T + T],
                 bass.AP(d_m, h * 128 * T, [[T, 128], [1, T]]))
        l3 = dma("sync", m2b[0:64, bo * T:bo * T + T],
                 bass.AP(d_m, h * 128 * T + 64 * T, [[T, 64], [1, T]]))
        l4 = dma("sync", m2b[64:128, bo * T:bo * T + T],
                 bass.AP(d_m, h * 128 * T, [[T, 64], [1, T]]))
        lv = None
        for grp in range(4):
            kc0 = grp * 8
            dma("sync", vb4[:, bo, kc0:kc0 + 8, 0:64],
                bass.AP(d_vhi, kc0 * 128 * 512 + h * 64,
                        [[512, 128], [128 * 512, 8], [1, 64]]))
            lv = dma("sync", vb4[:, bo, kc0:kc0 + 8, 65:129],
                     bass.AP(d_vlo, kc0 * 128 * 512 + h * 64,
                             [[512, 128], [128 * 512, 8], [1, 64]]))
        head_ld = [l1, l2, l3, l4, lv, tok_vo0, tok_vo1,
                   pe_tok[-1] if h == 0 else None]

        last_pv = None
        for s in range(NSPAN):
            for kc in range(4 * s + 4):
                n = h * 144 + CUMS[s] + kc
                pss = (ps_a, ps_b)[kc % 2]
                mm([ex_tok[n - 2] if n >= 2 else None] +
                   (head_ld if (s == 0 and kc == 0) else []) +
                   ([FW(w) for w in (l1, l2, l3, l4, lv)] if kc == 0 else []),
                   pss[:], kbb[:, bo * T + kc * 128:bo * T + kc * 128 + 128],
                   mb[:, bo * T + s * 512:bo * T + s * 512 + 512], True, False)
                m2_ = mm([], pss[:],
                         kbb[:, bo * T + kc * 128:bo * T + kc * 128 + 128],
                         m2b[:, bo * T + s * 512:bo * T + s * 512 + 512],
                         False, True, sem="t_sm")
                rg = [dec_tok[n - 3][0], dec_tok[n - 3][1]] if n >= 3 else []
                slot = n % 3
                ex = op("scalar", lambda a, slot=slot, pss=pss, h=h, kc=kc: a.activation(
                    pfr[:, slot * 512:slot * 512 + 512], pss[:], ACTF.Exp,
                    bias=bjt[:, h * 32 + kc:h * 32 + kc + 1], scale=1.0),
                    [m2_, tok_bj] + rg, sem="a_ex")
                ex_tok.append(ex)
                mk = ex
                if kc >= 4 * s:
                    col = (kc - 4 * s) * 128
                    mk = op("gpsimd", lambda g, slot=slot, col=col: g.affine_select(
                        pfr[:, slot * 512 + col:slot * 512 + col + 128],
                        pfr[:, slot * 512 + col:slot * 512 + col + 128],
                        [[1, 128]], ALU.is_ge, 0.0, base=0, channel_multiplier=-1),
                        [ex], sem="p_mk")
                pc1 = op("vector", lambda v, slot=slot, kc=kc: v.tensor_copy(
                    pthi[:, kc * 512:kc * 512 + LS],
                    pfr[:, slot * 512:slot * 512 + LS]),
                    [mk], sem="v_pd")
                vl = op("vector", lambda v, slot=slot, kc=kc: v.scalar_tensor_tensor(
                    ptlo[:, kc * 512:kc * 512 + LS],
                    pfr[:, slot * 512:slot * 512 + LS], 1.0,
                    pthi[:, kc * 512:kc * 512 + LS], ALU.mult, ALU.subtract),
                    [FW(pc1)], sem="v_pd")
                pc2 = op("gpsimd", lambda g, slot=slot, kc=kc: g.tensor_copy(
                    pthi[:, kc * 512 + LS:kc * 512 + 512],
                    pfr[:, slot * 512 + LS:slot * 512 + 512]),
                    [mk], sem="p_pd")
                pl = op("gpsimd", lambda g, slot=slot, kc=kc: g.tensor_sub(ptlo[:, kc * 512 + LS:kc * 512 + 512],
                    pfr[:, slot * 512 + LS:slot * 512 + 512], pthi[:, kc * 512 + LS:kc * 512 + 512]),
                    [FW(pc2)], sem="p_pd")
                dec_tok.append((vl, pl))
            if d_pdbg is not None and h == 0 and s == 7:
                dma("gpsimd", d_pdbg.ap(), pthi[:, 0:NKC * 512],
                    [dec_tok[-1][0], dec_tok[-1][1]])
            for qi in range(4):
                qt = 4 * s + qi
                need = h * 144 + CUMS[s] + qt
                po = (ps_c, ps_d)[qt % 2]
                nm_gate = nm_tok.get((h, qt - 2)) if qt >= 2 else (
                    nm_tok.get((h - 1, 31)) if h >= 1 else None)
                first = True
                lastm = None
                for kc in range(qt + 1):
                    lastm = mm([dec_tok[need][0], dec_tok[need][1], nm_gate]
                               if first else [],
                               bass.AP(po, 0, [[512, 128], [1, 129]]),
                               pthi[:, kc * 512 + qi * 128:kc * 512 + qi * 128 + 128],
                               vbb[:, bo * NKC * 129 + kc * 129:bo * NKC * 129 + kc * 129 + 129],
                               kc == 0, False)
                    first = False
                    lastm = mm([], bass.AP(po, 0, [[512, 128], [1, 65]]),
                               ptlo[:, kc * 512 + qi * 128:kc * 512 + qi * 128 + 128],
                               vbb[:, bo * NKC * 129 + kc * 129:bo * NKC * 129 + kc * 129 + 65],
                               False, kc == qt,
                               sem="t_pv" if kc == qt else None)
                st_gate = ost_tok.get((h, qt - 2)) if qt >= 2 else (
                    ost_tok.get((h - 1, 31)) if h >= 1 else None)
                prev_r3 = nm_tok.get((h, qt - 1)) if qt >= 1 else (
                    nm_tok.get((h - 1, 31)) if h >= 1 else None)
                c1 = op("vector", lambda v, po=po: v.tensor_copy(
                    otmp[:, 0:129],
                    bass.AP(po, 0, [[512, 128], [1, 129]])),
                    [lastm, st_gate, prev_r3], sem="v_nm")
                r1 = op("vector", lambda v: v.reciprocal(
                    nsc[:, 0:2], otmp[:, 64:66]), [FW(c1)], sem="v_nm")
                a1 = op("gpsimd", lambda g: g.tensor_add(
                    otmp[:, 129:193], otmp[:, 0:64], otmp[:, 65:129]),
                    [c1], sem="p_nm")
                r3 = op("gpsimd", lambda g, qt=qt: g.tensor_mul(
                    otile[:, (qt % 2) * 64:(qt % 2) * 64 + 64],
                    otmp[:, 129:193],
                    bass.AP(nsc, 0, [[8, 128], [0, 64]])),
                    [r1], sem="p_nm")
                nm_tok[(h, qt)] = r3
                if d_podbg is not None and h == 0 and qt in (0, 5):
                    pcp = op("vector", lambda v, po=po: v.tensor_copy(
                        pfr[:, 0:129], bass.AP(po, 0, [[512, 128], [1, 129]])),
                        [lastm], sem="v_nm")
                    dma("gpsimd",
                        bass.AP(d_podbg, (0 if qt == 0 else 1) * 128 * 129,
                                [[129, 128], [1, 129]]),
                        pfr[:, 0:129], [pcp])
                if d_vdbg is not None and h == 0 and qt == 0:
                    dma("gpsimd", d_vdbg.ap(), vbb[:], [lastm])
                if d_ldbg is not None:
                    lcp = op("vector", lambda v, po=po: v.tensor_copy(
                        nsc[:, 4:5], bass.AP(po, 64, [[512, 128], [1, 1]])),
                        [lastm], sem="v_nm")
                    dma("gpsimd", bass.AP(d_ldbg, (h * NTT + qt) * 128, [[1, 128], [1, 1]]),
                        nsc[:, 4:5], [lcp])
                ost_tok[(h, qt)] = dma(
                    "gpsimd",
                    bass.AP(d_o, qt * 128 * 512 + h * 64, [[512, 128], [1, 64]]),
                    otile[:, (qt % 2) * 64:(qt % 2) * 64 + 64], [r3])
                if qt == 31:
                    last_pv = lastm
        hd_tok[h] = last_pv
    p2_done = ost_tok[(HPC - 1, 31)]
    p3_gate = [hd_tok[HPC - 1], p2_done]

    wp_lo = []
    wp_hi = []
    for c in range(4):
        tl = dma("sync", wpstg[:, 0:1024],
                 bass.AP(d_wpT, c * 128 * E, [[E, 128], [1, E]]),
                 p3_gate + ([wp_lo[c - 1]] if c >= 1 else []))
        th = op("vector", lambda v, c=c: v.tensor_copy(
            wp_s[:, c * 1024:c * 1024 + 1024],
            wpstg[:, 0:1024]), [tl], sem="v_wp")
        tp_ = op("gpsimd", lambda g, c=c: g.tensor_sub(wp_s[:, 4096 + c * 1024:4096 + c * 1024 + 1024],
            wpstg[:, 0:1024], wp_s[:, c * 1024:c * 1024 + 1024]),
            [th], sem="p_wp")
        wp_lo.append(tp_)
        wp_hi.append(th)
    wp_ready = wp_lo[-1]

    od_tok = []
    oe_tok = []
    we_tok = []
    cc_st = []
    for qt in range(NTT):
        sl = qt % 2
        gates = p3_gate if qt == 0 else []
        if qt >= 2:
            gates = [od_tok[qt - 2][0], od_tok[qt - 2][1]]
        lo_ = dma("sync", orld[:, sl * 512:sl * 512 + 512],
                  bass.AP(d_o, qt * 128 * 512, [[512, 128], [1, 512]]), gates)
        oc1 = op("vector", lambda v, sl=sl: v.tensor_copy(
            obhi[:, sl * 512:sl * 512 + 256], orld[:, sl * 512:sl * 512 + 256]),
            [lo_], sem="v_od")
        vl = op("vector", lambda v, sl=sl: v.scalar_tensor_tensor(
            oblo[:, sl * 512:sl * 512 + 256], orld[:, sl * 512:sl * 512 + 256],
            1.0, obhi[:, sl * 512:sl * 512 + 256], ALU.mult, ALU.subtract),
            [FW(oc1)], sem="v_od")
        oc2 = op("gpsimd", lambda g, sl=sl: g.tensor_copy(
            obhi[:, sl * 512 + 256:sl * 512 + 512],
            orld[:, sl * 512 + 256:sl * 512 + 512]), [lo_], sem="p_od")
        pl = op("gpsimd", lambda g, sl=sl: g.tensor_sub(oblo[:, sl * 512 + 256:sl * 512 + 512],
            orld[:, sl * 512 + 256:sl * 512 + 512], obhi[:, sl * 512 + 256:sl * 512 + 512]),
            [FW(oc2)], sem="p_od")
        od_tok.append((vl, pl))
        prev_oe = oe_tok[2 * (qt - 1) + 1] if qt >= 1 else te_tok[-1]
        tr1 = None
        for c in range(4):
            tr1 = op("tensor", lambda t, c=c, sl=sl: t.transpose(
                ps_tr[:, c * 128:c * 128 + 128],
                obhi[:, sl * 512 + c * 128:sl * 512 + c * 128 + 128], ident[:]),
                ([vl, pl, prev_oe] if c == 0 else []) + [FW(vl), FW(pl)],
                sem="t_ot" if c == 3 else None)
        tr2 = None
        for c in range(4):
            tr2 = op("tensor", lambda t, c=c, sl=sl: t.transpose(
                ps_t2[:, c * 128:c * 128 + 128],
                oblo[:, sl * 512 + c * 128:sl * 512 + c * 128 + 128], ident[:]),
                [FW(vl), FW(pl)], sem="t_ot" if c == 3 else None)
        oe1 = op("scalar", lambda a, sl=sl: a.activation(
            oThi[:, sl * 512:sl * 512 + 512], ps_tr[:, 0:512], ACTF.Copy),
            [tr1], sem="a_oe")
        oe2 = op("scalar", lambda a, sl=sl: a.activation(
            oTlo[:, sl * 512:sl * 512 + 512], ps_t2[:, 0:512], ACTF.Copy),
            [tr2], sem="a_oe")
        oe_tok.extend([oe1, oe2])
        for os_ in range(2):
            iw = qt * 2 + os_
            pw = (ps_w, ps_u)[iw % 2]
            extra = []
            if iw == 0:
                extra = [ue_a[NSP - 1]]
            if iw == 1:
                extra = [ue_a[NSP - 1]]
            first = True
            lastm = None
            for c in range(4):
                cwv = [FW(wp_hi[c]), FW(wp_lo[c]), FW(oe1), FW(oe2)]
                for (osl, wsl) in PASS3:
                    lastm = mm(([oe1, oe2, wp_ready,
                                 we_tok[iw - 2] if iw >= 2 else None] + extra
                                if first else []) + cwv,
                               pw[:],
                               (oThi if osl == 0 else oTlo)[:, sl * 512 + c * 128:sl * 512 + c * 128 + 128],
                               wp_s[:, wsl * 4096 + c * 1024 + os_ * 512:wsl * 4096 + c * 1024 + os_ * 512 + 512],
                               first, (c == 3 and (osl, wsl) == (1, 0)),
                               sem="t_wm" if (c == 3 and (osl, wsl) == (1, 0)) else None)
                    cwv = []
                    first = False
            ve = op("vector", lambda v, iw=iw, pw=pw: v.tensor_copy(
                apev[:, (iw % 2) * 512:(iw % 2) * 512 + 512], pw[:]),
                [lastm, cc_st[iw - 2] if iw >= 2 else None], sem="v_we")
            we_tok.append(ve)
            cc_st.append(dma(
                "gpsimd",
                bass.AP(d_ccin, qt * 128 * E + os_ * 512, [[E, 128], [1, 512]]),
                apev[:, (iw % 2) * 512:(iw % 2) * 512 + 512], [ve]))
    cc_tok = op("gpsimd", lambda g: g.collective_compute(
        "ReduceScatter", ALU.add, ins=[d_ccin.ap()], outs=[d_ccout.ap()],
        replica_groups=REPL), [cc_st[-1]], sem="c_cc")
    if d_ccdbg is not None:
        dma("gpsimd", d_ccdbg.ap(), d_ccout.ap(), [cc_tok])

    fe_tok = []
    inb_tok = {}
    hsp = {}

    def in_build(src, f, stage, waits_first, ssl, extra_tr=()):
        fsl = 0
        idx = len(inb_tok)
        rg = [fe_tok[2 * (idx - 1) + 1]] if idx >= 1 else []
        c1 = op("vector", lambda v, ssl=ssl, src=src: v.tensor_copy(
            fdhi[:, 0:512], src[:, ssl:ssl + 512]),
            list(waits_first) + rg, sem="v_fd")
        vl = op("vector", lambda v, ssl=ssl, src=src: v.scalar_tensor_tensor(
            fdlo[:, 0:512], src[:, ssl:ssl + 512], 1.0,
            fdhi[:, 0:512], ALU.mult, ALU.subtract), [FW(c1)], sem="v_fd")
        c2 = op("gpsimd", lambda g, ssl=ssl, src=src: g.tensor_copy(
            fdhi[:, 512:1024], src[:, ssl + 512:ssl + 1024]),
            list(waits_first) + rg, sem="p_fd")
        pl = op("gpsimd", lambda g, ssl=ssl, src=src: g.tensor_sub(fdlo[:, 512:1024], src[:, ssl + 512:ssl + 1024], fdhi[:, 512:1024]),
            [FW(c2)], sem="p_fd")
        prev_e = fe_tok[2 * (idx - 1) + 1] if idx >= 1 else None
        tr1 = None
        for c in range(8):
            tr1 = op("tensor", lambda t, c=c: t.transpose(
                ps_tr[:, c * 128:c * 128 + 128],
                fdhi[:, c * 128:c * 128 + 128], ident[:]),
                ([vl, pl, prev_e] + list(extra_tr) if c == 0 else [])
                + [FW(vl), FW(pl)],
                sem="t_ft" if c == 7 else None)
        tr2 = None
        for c in range(8):
            tr2 = op("tensor", lambda t, c=c: t.transpose(
                ps_t2[:, c * 128:c * 128 + 128],
                fdlo[:, c * 128:c * 128 + 128], ident[:]),
                [FW(vl), FW(pl)], sem="t_ft" if c == 7 else None)
        e1 = op("scalar", lambda a, f=f: a.activation(
            inT_hi[:].rearrange("p (c w) -> p c w", w=2048)[:, :, f * 128:f * 128 + 128],
            ps_tr[:].rearrange("p (c w) -> p c w", w=128), ACTF.Copy),
            [tr1], sem="a_fe")
        e2 = op("scalar", lambda a, f=f: a.activation(
            inT_lo[:].rearrange("p (c w) -> p c w", w=2048)[:, :, f * 128:f * 128 + 128],
            ps_t2[:].rearrange("p (c w) -> p c w", w=128), ACTF.Copy),
            [tr2], sem="a_fe")
        fe_tok.extend([e1, e2])
        inb_tok[(stage, f)] = (vl, pl, e2)
        return e2

    tok_ufz = op("gpsimd", lambda g: g.memset(uf[:], 0.0), [cc_st[-1]], sem="p_set")
    tok_lbz = op("gpsimd", lambda g: g.memset(lbT_s[:], 0.0), [], sem="p_set")
    fxc = [None] * NTTH
    for f in range(NTTH):
        lx = dma("sync", xrld[:, 0:1024],
                 bass.AP(d_xown, f * 128 * E, [[E, 128], [1, E]]),
                 [cc_st[-1] if f == 0 else None,
                  fxc[f - 1] if f >= 1 else None])
        out_gates = []
        if f >= 1:
            pv = inb_tok[(0, f - 1)]
            out_gates = [hsp[(0, f - 1)], pv[0], pv[1]]
        xc, xn = ln_emit(xrld[:, 0:1024],
                         hring[:, 0:1024],
                         fsq[:, :], fln, f % 2, [lx], "f0", out_gates)
        fxc[f] = xc
        hsp[(0, f)] = dma("sync", bass.AP(d_h, f * 128 * E, [[E, 128], [1, E]]),
                          hring[:, 0:1024], [xn])
        in_build(hring, f, 0, [xn], 0,
                 extra_tr=[oe_tok[-1]] if f == 0 else [])

    dq_tok = []
    ye_tok = {}
    ep_tok = {}
    fm_tok = {}
    fu_a = []
    fu_p = []
    fin_tok = [None] * NTTH
    outst = [None] * NTTH
    lgr = lbe = None
    gl_tok = {}

    for L in range(6):
        pend_build = []
        blk = L // 2
        wslot = L % 2
        t1prev = [None]
        for dc in range(8):
            gidx = L * 8 + dc
            dsl = (gidx % 2) * 1024
            lq = dma("sync", qstg[:, 0:1024],
                     bass.AP(d_qT, L * E * E + dc * 128 * E, [[E, 128], [1, E]]),
                     (p3_gate if gidx == 0 else []) +
                     [t1prev[0],
                      fm_tok.get((L - 2, NTTH - 1)) if (L >= 2 and dc == 0) else None])
            lsc = dma("sync", screp[:, dsl:dsl + 1024],
                      bass.AP(d_scT, (L * E + dc * 128) * E, [[E, 128], [1, E]]),
                      [dq_tok[gidx - 2][0] if gidx >= 2 else None])
            t1 = op("vector", lambda v, dsl=dsl: v.tensor_scalar(
                wdeq[:, dsl:dsl + 1024], qstg[:, 0:1024],
                1.0 / 7.5, -1.0, ALU.mult, ALU.add),
                [lq, dq_tok[gidx - 2][1] if gidx >= 2 else None], sem="v_dq")
            t1prev[0] = t1
            t2 = op("vector", lambda v, dsl=dsl: v.tensor_mul(
                wdeq[:, dsl:dsl + 1024], wdeq[:, dsl:dsl + 1024],
                screp[:, dsl:dsl + 1024]),
                [lsc, FW(t1)], sem="v_dq")
            t3 = op("vector", lambda v, dc=dc, wslot=wslot, dsl=dsl: v.tensor_copy(
                wf[:, wslot * 16384 + dc * 1024:wslot * 16384 + dc * 1024 + 1024],
                wdeq[:, dsl:dsl + 1024]),
                [FW(t2)], sem="v_dq")
            t4 = op("gpsimd", lambda g, dc=dc, wslot=wslot, dsl=dsl: g.tensor_sub(
                wf[:, wslot * 16384 + 8192 + dc * 1024:wslot * 16384 + 8192 + dc * 1024 + 1024],
                wdeq[:, dsl:dsl + 1024],
                wf[:, wslot * 16384 + dc * 1024:wslot * 16384 + dc * 1024 + 1024]),
                [t3], sem="p_dq")
            dq_tok.append((t3, t4))
        dq_ready = dq_tok[-1]
        if d_wfdbg is not None and L == dbg_l:
            dma("gpsimd", d_wfdbg.ap(), wf[:, 0:16384],
                [dq_ready[0], dq_ready[1]])
        u_prev = fu_a[(L - 1) * 4 + 3] if L >= 1 else None
        la_t = []
        for c in range(8):
            ll = dma("sync", lastg[:, (c % 2) * 32:(c % 2) * 32 + 32],
                     bass.AP(d_laT, L * E * 32 + c * 128 * 32, [[32, 128], [1, 32]]),
                     [la_t[c - 2][0] if c >= 2 else u_prev,
                      la_t[c - 2][1] if c >= 2 else None])
            lh = op("vector", lambda v, c=c: v.tensor_copy(
                laT_s[:, c * 32:c * 32 + 32],
                lastg[:, (c % 2) * 32:(c % 2) * 32 + 32]), [ll], sem="v_lw")
            lo2 = op("gpsimd", lambda g, c=c: g.tensor_sub(laT_s[:, 256 + c * 32:256 + c * 32 + 32],
                lastg[:, (c % 2) * 32:(c % 2) * 32 + 32], laT_s[:, c * 32:c * 32 + 32]),
                [lh], sem="p_lw")
            la_t.append((lh, lo2))
        lbh = lbl = None
        for qh in range(4):
            llb = dma("sync", lbstg[:, :],
                      bass.AP(d_lbT, L * 32 * E + qh * 256, [[E, 32], [1, 256]]),
                      [fm_tok.get((L - 1, NTTH - 1)) if (L >= 1 and qh == 0) else None,
                       lbl])
            lbh = op("vector", lambda v, qh=qh: v.tensor_copy(
                lbT_s[0:32, qh * 256:qh * 256 + 256], lbstg[:, :]),
                [llb, tok_lbz], sem="v_lw")
            lbl = op("gpsimd", lambda g, qh=qh: g.tensor_sub(
                lbT_s[0:32, 1024 + qh * 256:1024 + qh * 256 + 256], lbstg[:, :],
                lbT_s[0:32, qh * 256:qh * 256 + 256]), [lbh], sem="p_lw")
        lora_ready = [la_t[-1][0], la_t[-1][1], lbh, lbl]
        lbr = dma("sync", brep[:, 0:1024],
                  bass.AP(d_bias, L * 128 * E, [[E, 128], [1, E]]),
                  [ye_tok.get((L - 1, NTTH - 1)) if L >= 1 else None])
        if L % 2 == 0:
            gb_gate = [gl_tok.get((L - 2, NTTH - 1)) if L >= 2 else None]
            lgr = dma("sync", grep[:, :], bass.AP(d_gam, blk * 128 * E, [[E, 128], [1, E]]),
                      gb_gate)
            lbe = dma("sync", berep[:, :], bass.AP(d_bet, blk * 128 * E, [[E, 128], [1, E]]),
                      [])
        for su in range(4):
            in_ready = fe_tok[2 * (L * NTTH + 4 * su + 3) + 1]
            iu = L * 4 + su
            extra = [we_tok[-1]] if iu == 0 else []
            first = True
            lastm = None
            for c in range(8):
                for (lsl, isl) in PASS3:
                    lastm = mm(([in_ready, fu_a[iu - 1] if iu >= 1 else None,
                                 *lora_ready] + extra if first else [])
                               + ([FW(in_ready)] if (lsl, isl) == PASS3[0] else []),
                               bass.AP(ps_u, 0, [[512, 32], [1, 512]]),
                               laT_s[:, lsl * 256 + c * 32:lsl * 256 + c * 32 + 32],
                               (inT_hi if isl == 0 else inT_lo)[:, c * 2048 + su * 512:c * 2048 + su * 512 + 512],
                               first, (c == 7 and (lsl, isl) == (1, 0)),
                               sem="t_fu" if (c == 7 and (lsl, isl) == (1, 0)) else None)
                    first = False
            au = op("scalar", lambda a: a.activation(
                ufstg[:, :], bass.AP(ps_u, 0, [[512, 32], [1, 512]]), ACTF.Copy),
                [lastm, fu_p[iu - 1] if iu >= 1 else None], sem="a_fu")
            vu = op("vector", lambda v, su=su: v.tensor_copy(
                uf[0:32, su * 512:su * 512 + 512], ufstg[:, :]),
                [au, tok_ufz], sem="v_fu")
            pu = op("gpsimd", lambda g, su=su: g.tensor_sub(uf[0:32, 2048 + su * 512:2048 + su * 512 + 512], ufstg[:, :], uf[0:32, su * 512:su * 512 + 512]),
                [vu], sem="p_fu")
            fu_a.append(au)
            fu_p.append(pu)
            if d_fdbg is not None and L == dbg_l:
                dtok = dma("gpsimd", bass.AP(d_fdbg, (2 * TH + su * 128) * E,
                                             [[E, 32], [1, 512]]),
                           ufstg[:, :], [vu, pu])
                fu_p[-1] = dtok
        u_ready = [fu_a[-1], fu_p[-1]]

        for f in range(NTTH):
            fsl = (f % 2) * 1024
            psg = [ye_tok.get((L, f - 2)) if f >= 2 else
                   (ye_tok.get((L - 1, NTTH - 2 + f)) if L >= 1 else None)]
            if L == 0 and f < 2:
                psg = [ex_tok[-1], nm_tok[(HPC - 1, 31)]]
            lastm = None
            for os_ in range(2):
                pb = PRING[(f % 2) * 2 + os_]
                first = True
                for c in range(8):
                    cw = [dq_tok[L * 8 + c][0], dq_tok[L * 8 + c][1],
                          FW(fe_tok[2 * (L * NTTH + f) + 1])]
                    for (isl, wsl) in PASS3:
                        lastm = mm((psg + [*u_ready,
                                           fe_tok[2 * (L * NTTH + f) + 1]]
                                    if first else []) + cw,
                                   pb[:],
                                   (inT_hi if isl == 0 else inT_lo)[:, c * 2048 + f * 128:c * 2048 + f * 128 + 128],
                                   wf[:, wslot * 16384 + wsl * 8192 + c * 1024 + os_ * 512:wslot * 16384 + wsl * 8192 + c * 1024 + os_ * 512 + 512],
                                   first, False)
                        cw = []
                        first = False
                for pi, (usl, lsl) in enumerate(PASS3):
                    lw = [FW(lbh), FW(lbl),
                          FW(fu_a[L * 4 + f // 4]),
                          FW(fu_p[L * 4 + f // 4])] if pi == 0 else []
                    lastm = mm(lw,
                               pb[:],
                               uf[:, usl * 2048 + f * 128:usl * 2048 + f * 128 + 128],
                               lbT_s[:, lsl * 1024 + os_ * 512:lsl * 1024 + os_ * 512 + 512],
                               False, pi == 2,
                               sem="t_fm" if (pi == 2 and os_ == 1) else None)
            fm_tok[(L, f)] = lastm
            ev_gates = [lastm, lbr]
            if f >= 2:
                pv = inb_tok.get((L + 1, f - 2))
                ev_gates += [ep_tok.get((L, f - 2)),
                             hsp.get((blk + 1, f - 2)) if L % 2 == 1 else None,
                             pv[1] if pv else None]
            elif L >= 1:
                pv = inb_tok.get((L, NTTH - 2 + f))
                ev_gates += [ep_tok.get((L - 1, NTTH - 2 + f)),
                             hsp.get(((L - 1) // 2 + 1, NTTH - 2 + f)) if (L - 1) % 2 == 1 else None,
                             pv[1] if pv else None]
            ev1 = op("vector", lambda v, f=f, fsl=fsl, wslot=wslot: v.scalar_tensor_tensor(
                yev[:, fsl:fsl + 512], PRING[(f % 2) * 2][:], 1.0,
                brep[:, 0:512], ALU.mult, ALU.add),
                ev_gates, sem="v_ye")
            ev2 = op("vector", lambda v, f=f, fsl=fsl, wslot=wslot: v.scalar_tensor_tensor(
                yev[:, fsl + 512:fsl + 1024], PRING[(f % 2) * 2 + 1][:], 1.0,
                brep[:, 512:1024], ALU.mult, ALU.add),
                [FW(lbr)], sem="v_ye")
            ye_tok[(L, f)] = ev2
            if d_fdbg is not None and L == dbg_l:
                ye_tok[(L, f)] = dma("gpsimd",
                                     bass.AP(d_fdbg, f * 128 * E, [[E, 128], [1, E]]),
                                     yev[:, fsl:fsl + 1024], [ev2])

            if L % 2 == 0:
                zgates = []
                if L * NTTH + f >= 2:
                    glp = gl_tok.get((L, f - 2)) or gl_tok.get((L - 2, NTTH - 2 + f))
                    pv = inb_tok.get((L + 1, f - 2)) or inb_tok.get((L - 1, NTTH - 2 + f))
                    zgates = [glp, pv[1] if pv else None]
                zgates = list(zgates) + [gl_tok.get((L, f - 1)), FW(ev2)]
                zc, zn = ln_emit(yev[:, fsl:fsl + 1024], zbuf[:, 0:1024],
                                 fsq[:, :], fln, f % 2, [ev2], "fl", zgates)
                gmul = op("vector", lambda v: v.tensor_mul(
                    zbuf[:, 0:1024], zbuf[:, 0:1024], grep[:, :]),
                    [lgr, zn], sem="v_ep")
                zb = op("vector", lambda v: v.tensor_add(
                    zbuf[:, 0:1024], zbuf[:, 0:1024], berep[:, :]),
                    [lbe, FW(gmul)], sem="v_ep")
                dbg_g = []
                if d_lndbg is not None and L == 0:
                    dbg_g.append(dma("gpsimd",
                                     bass.AP(d_lndbg, f * 128 * E, [[E, 128], [1, E]]),
                                     zbuf[:, 0:1024], [zb]))
                    dbg_g.append(dma("gpsimd",
                                     bass.AP(d_lnst, f * 128 * 16, [[16, 128], [1, 16]]),
                                     fln[:, :], [zn]))
                    if f == 0:
                        dbg_g.append(dma("gpsimd",
                                         bass.AP(d_lndbg, 16 * 128 * E, [[E, 128], [1, E]]),
                                         grep[:, :], [lgr]))
                        dbg_g.append(dma("gpsimd",
                                         bass.AP(d_lndbg, 17 * 128 * E, [[E, 128], [1, E]]),
                                         berep[:, :], [lbe]))
                ggate = []
                pv = inb_tok.get((L + 1, f - 2))
                if pv:
                    ggate = [pv[0], pv[1]]
                gl = op("scalar", lambda a, fsl=fsl: a.activation(
                    yev[:, fsl:fsl + 1024], zbuf[:, 0:1024], ACTF.Gelu),
                    [zb, zc, ye_tok[(L, f)] if ye_tok[(L, f)] is not ev2 else None]
                    + ggate + dbg_g, sem="a_gl")
                gl_tok[(L, f)] = gl
                ep_tok[(L, f)] = gl
                if d_fdbg is not None and L == dbg_l:
                    gl = dma("gpsimd", bass.AP(d_fdbg, (TH + f * 128) * E, [[E, 128], [1, E]]),
                             yev[:, fsl:fsl + 1024], [gl])
                if d_fdbg is not None and L == 2 and False:
                    pass
                pend_build.append((yev, f, L + 1, gl))
            else:
                if f >= 1:
                    hring_gate = [ep_tok[(L, f - 1)]]
                elif L == 1:
                    hring_gate = [inb_tok[(0, NTTH - 1)][0],
                                  inb_tok[(0, NTTH - 1)][1], hsp[(0, NTTH - 1)]]
                else:
                    hring_gate = [ep_tok[(L - 2, NTTH - 1)]]
                lh_ = dma("sync", hring[:, 0:1024],
                          bass.AP(d_h, blk * TH * E + f * 128 * E, [[E, 128], [1, E]]),
                          [hsp.get((blk, f))] + hring_gate)
                hn = op("vector", lambda v, fsl=fsl: v.tensor_add(
                    yev[:, fsl:fsl + 1024], yev[:, fsl:fsl + 1024],
                    hring[:, 0:1024]),
                    [ev2, lh_, ye_tok[(L, f)] if ye_tok[(L, f)] is not ev2 else None],
                    sem="v_ep")
                ep_tok[(L, f)] = hn
                if L < 5:
                    hsp[(blk + 1, f)] = dma(
                        "gpsimd",
                        bass.AP(d_h, (blk + 1) * TH * E + f * 128 * E, [[E, 128], [1, E]]),
                        yev[:, fsl:fsl + 1024], [hn])
                    pend_build.append((yev, f, L + 1, hn))
                else:
                    lx2 = dma("sync", xrld[:, 0:1024],
                              bass.AP(d_xown, f * 128 * E, [[E, 128], [1, E]]),
                              [fin_tok[f - 1] if f >= 1 else fxc[NTTH - 1]])
                    lcc = dma("sync", ccrld[:, 0:1024],
                              bass.AP(d_ccout, f * 128 * E, [[E, 128], [1, E]]),
                              [cc_tok if f == 0 else fin_tok[f - 1],
                               gl_tok[(4, NTTH - 1)] if f == 0 else None])
                    o1 = op("vector", lambda v, fsl=fsl: v.tensor_add(
                        outb[:, 0:1024], yev[:, fsl:fsl + 1024],
                        xrld[:, 0:1024]),
                        [hn, lx2, outst[f - 1] if f >= 1 else None], sem="v_fin")
                    o2 = op("vector", lambda v, fsl=fsl: v.tensor_add(
                        outb[:, 0:1024], outb[:, 0:1024],
                        ccrld[:, 0:1024]), [lcc, FW(o1)], sem="v_fin")
                    fin_tok[f] = o2
                    outst[f] = dma("sync",
                                   bass.AP(d_out, f * 128 * E, [[E, 128], [1, E]]),
                                   outb[:, 0:1024], [o2])
            while len(pend_build) > 1:
                a_ = pend_build.pop(0)
                in_build(a_[0], a_[1], a_[2], [a_[3]], (a_[1] % 2) * 1024)
        while pend_build:
            a_ = pend_build.pop(0)
            in_build(a_[0], a_[1], a_[2], [a_[3]], (a_[1] % 2) * 1024)

    waits_fin = [("q_sync", G.nsem.get("q_sync", 0))]
    if "q_gpsimd" in G.nsem:
        waits_fin.append(("q_gpsimd", G.nsem["q_gpsimd"]))
    op("gpsimd", lambda g: g.nop(), waits_fin)

    semnames = set(G.nsem)
    for lst in G.ops.values():
        for _, waits, _t in lst:
            for w in waits:
                semnames.add(w[0])
    sems = {sn: nc.semaphore(sn).__enter__() for sn in sorted(semnames)}

    with nc.Block() as block:
        @block.sync
        def _(e):
            G.emit(e, "sync", sems)

        @block.gpsimd
        def _(e):
            G.emit(e, "gpsimd", sems)

        @block.tensor
        def _(e):
            G.emit(e, "tensor", sems)

        @block.scalar
        def _(e):
            G.emit(e, "scalar", sems)

        @block.vector
        def _(e):
            G.emit(e, "vector", sems)

    return nc


def _prep_inputs(inputs):
    x = np.asarray(inputs["L0_attn_x"], dtype=np.float32)
    wq = np.asarray(inputs["L0_attn_Wq"], dtype=np.float32)
    wk = np.asarray(inputs["L0_attn_Wk"], dtype=np.float32)
    wv = np.asarray(inputs["L0_attn_Wv"], dtype=np.float32)
    wp = np.asarray(inputs["L0_attn_Wp"], dtype=np.float32)
    aq = np.asarray(inputs["L0_attn_Aq"], dtype=np.float32)
    ak = np.asarray(inputs["L0_attn_Ak"], dtype=np.float32)
    av = np.asarray(inputs["L0_attn_Av"], dtype=np.float32)
    bq = np.asarray(inputs["L0_attn_Bq"], dtype=np.float32)
    bk = np.asarray(inputs["L0_attn_Bk"], dtype=np.float32)
    bv = np.asarray(inputs["L0_attn_Bv"], dtype=np.float32)
    qw = np.asarray(inputs["L0_ffn_qweights"], dtype=np.int32)
    sc = np.asarray(inputs["L0_ffn_scales"], dtype=np.float32)
    bi = np.asarray(inputs["L0_ffn_biases"], dtype=np.float32)
    la = np.asarray(inputs["L0_ffn_lora_a"], dtype=np.float32)
    lb = np.asarray(inputs["L0_ffn_lora_b"], dtype=np.float32)
    gm = np.asarray(inputs["L0_ffn_ln_gamma"], dtype=np.float32)
    bt = np.asarray(inputs["L0_ffn_ln_beta"], dtype=np.float32)

    wqT = np.ascontiguousarray(wq.T)
    wkT = np.ascontiguousarray(wk.T)
    wvT = np.ascontiguousarray(wv.T)
    wpT = np.ascontiguousarray(wp.T)
    bqkvT = np.ascontiguousarray(np.concatenate([bq.T, bk.T, bv.T], axis=1))
    qT6 = np.ascontiguousarray(qw.transpose(0, 2, 1))
    scT6 = np.ascontiguousarray(np.repeat(sc.transpose(0, 2, 1), 16, axis=1))
    laT6 = np.ascontiguousarray(la.transpose(0, 2, 1))
    lbT6 = np.ascontiguousarray(lb.transpose(0, 2, 1))

    start = 2 ** (-2 ** (-(math.log2(16) - 3)))
    slopes = np.array([start * start ** i for i in range(16)], dtype=np.float32)

    maps = []
    for c in range(8):
        b, hh = c // 2, c % 2
        cols = slice(hh * 512, hh * 512 + 512)
        aqkvT = np.ascontiguousarray(
            np.stack([aq.T[:, cols], ak.T[:, cols], av.T[:, cols]], axis=0))
        maps.append({
            "x": np.ascontiguousarray(x[b]),
            "xown": np.ascontiguousarray(x[b, hh * TH:(hh + 1) * TH]),
            "wqT": np.ascontiguousarray(wqT[:, cols]),
            "wkT": np.ascontiguousarray(wkT[:, cols]),
            "wvT": np.ascontiguousarray(wvT[:, cols]),
            "bqkvT": bqkvT,
            "aqkvT": aqkvT,
            "wpT": np.ascontiguousarray(wpT[hh * 512:hh * 512 + 512, :]),
            "qT6": qT6,
            "scT6": scT6,
            "bias6": np.ascontiguousarray(np.broadcast_to(bi[:, None, :], (6, 128, E))),
            "laT6": laT6,
            "lbT6": lbT6,
            "gam": np.ascontiguousarray(np.broadcast_to(gm[:, None, :], (3, 128, E))),
            "bet": np.ascontiguousarray(np.broadcast_to(bt[:, None, :], (3, 128, E))),
            "nslope": np.ascontiguousarray(np.broadcast_to(
                (-slopes[hh * 8:hh * 8 + 8]).reshape(1, 8), (128, 8))),
        })
    return maps


_NC_CACHE = []


def kernel(**inputs):
    maps = _prep_inputs(inputs)
    if not _NC_CACHE:
        _NC_CACHE.append(build_nc())
    nc = _NC_CACHE[0]
    res = run_bass_kernel_spmd(nc, maps, core_ids=list(range(8)))
    out = np.empty((4, T, E), dtype=np.float32)
    for c in range(8):
        b, hh = c // 2, c % 2
        out[b, hh * TH:(hh + 1) * TH] = res.results[c]["out"]
    return out
